# version 06
# baseline (speedup 1.0000x reference)
import numpy as np
import concourse.bass as bass
import concourse.mybir as mybir
from concourse.bass_utils import run_bass_kernel_spmd

F32 = mybir.dt.float32
I32 = mybir.dt.int32
AF = mybir.ActivationFunctionType
ALU = mybir.AluOpType
AXX = mybir.AxisListType.X

H = 2048
S = 4096
T = 1024
NCH = S // T
NB = H // 128
FC = 8192
NOG = FC // 128
SHRINKS = np.linspace(0.4, 1.0, 16)
QMAX = 15.0
TH = T // 512


class Sem:
    def __init__(self, nc, name):
        self.h = nc.alloc_semaphore(name=name)
        self.n = 0


class Ctx:
    ENGS = ("sync", "tensor", "vector", "scalar", "gpsimd")

    def __init__(self, nc):
        self.nc = nc
        self.plan = {e: [] for e in self.ENGS}
        self.sq = Sem(nc, "sq")
        self.gq = Sem(nc, "gq")
        self.sp = Sem(nc, "sp")
        self.sv = Sem(nc, "sv")
        self.sa = Sem(nc, "sa")
        self.sg = Sem(nc, "sg")
        self.tok = Sem(nc, "tok")

    def emit(self, eng, fn):
        self.plan[eng].append(fn)

    def wait(self, eng, sem, val=None):
        v = sem.n if val is None else val
        if v > 0:
            self.emit(eng, lambda e, s=sem.h, v=v: e.wait_ge(s, v))

    def op(self, eng, fn, sem=None):
        if sem is None:
            sem = {"tensor": self.sp, "vector": self.sv, "scalar": self.sa,
                   "gpsimd": self.sg}[eng]
        sem.n += 1
        self.emit(eng, lambda e, f=fn, s=sem.h: f(e).then_inc(s, 1))

    def op_(self, eng, fn):
        self.emit(eng, fn)

    def dma(self, eng, out, in_, sem=None):
        if sem is None:
            sem = self.sq if eng == "sync" else self.gq
        sem.n += 16
        self.emit(eng, lambda e, o=out, i=in_, s=sem.h: e.dma_start(out=o, in_=i).then_inc(s, 16))

    def s_op(self, eng, fn):
        w0 = self.tok.n
        self.tok.n += 1

        def run(e, f=fn, w0=w0, s=self.tok.h):
            if w0:
                e.wait_ge(s, w0)
            f(e).then_inc(s, 1)
        self.emit(eng, run)

    def s_dma(self, eng, out, in_):
        w0 = self.tok.n
        self.tok.n += 16

        def run(e, o=out, i=in_, w0=w0, s=self.tok.h):
            if w0:
                e.wait_ge(s, w0)
            e.dma_start(out=o, in_=i).then_inc(s, 16)
        self.emit(eng, run)

    def s_join(self):
        for eng in self.ENGS:
            self.emit(eng, lambda e, v=self.tok.n, s=self.tok.h: e.wait_ge(s, v))

    def ctx_barrier(self):
        for eng in self.ENGS:
            for sem in (self.sq, self.gq, self.sp, self.sv, self.sa, self.sg):
                self.wait(eng, sem)


def wslice_in(w, c0, ncol):
    return w[:, c0:c0 + ncol].rearrange("(kb p) r -> p kb r", p=128)


def build(nc, taps=None):
    def IN(name, shape):
        return nc.dram_tensor(name, list(shape), F32, kind="ExternalInput").ap()

    x_in = IN("x", (H, S))
    wf, wi, wv, wq = IN("wf", (H, H)), IN("wi", (H, H)), IN("wv", (H, H)), IN("wq", (H, H))
    wro, wup, wgate, wdn = IN("wro", (H, H)), IN("wup", (H, H)), IN("wgate", (H, H)), IN("wdn", (H, H))
    wfc, wpj = IN("wfc", (FC, H)), IN("wpj", (H, FC))
    wxT, wdtT = IN("wxT", (H, 96)), IN("wdtT", (64, H))
    bdt_i, alog_i = IN("bdt", (128, 16)), IN("alog", (128, 256))
    dvec_i, gam_i, i0_i = IN("dvec", (128, 16)), IN("gam", (128, 16)), IN("i0", (128, 16))
    w1T = IN("w1T", (H, H))
    ident_i = IN("ident", (128, 128))
    e96_i = IN("e96", (96, 2048))

    out_d = nc.dram_tensor("out", [H, S], F32, kind="ExternalOutput").ap()

    s_x1 = nc.dram_tensor("s_x1", [H, T], F32, kind="Internal").ap()
    s_x2 = nc.dram_tensor("s_x2", [H, T], F32, kind="Internal").ap()
    s_x3 = nc.dram_tensor("s_x3", [H, T], F32, kind="Internal").ap()
    s_rd = nc.dram_tensor("s_rd", [NB, 128, T], F32, kind="Internal").ap()
    s_h2 = nc.dram_tensor("s_h2", [NOG, 128, T], F32, kind="Internal").ap()
    s_dfc = nc.dram_tensor("s_dfc", [H, FC], F32, kind="Internal").ap()
    s_dpj = nc.dram_tensor("s_dpj", [FC, H], F32, kind="Internal").ap()
    s_dw1 = nc.dram_tensor("s_dw1", [H, H], F32, kind="Internal").ap()
    s_bc = nc.dram_tensor("s_bc", [16, T], F32, kind="Internal").ap()
    s_cols = nc.dram_tensor("s_cols", [16, 128], F32, kind="Internal").ap()

    tap_d = {}
    if taps:
        for nm, shape in taps:
            tap_d[nm] = nc.dram_tensor("tap_" + nm, list(shape), F32, kind="ExternalOutput").ap()

    def SB(name, shape, dtype=F32):
        return nc.alloc_sbuf_tensor(name, list(shape), dtype).ap()

    BA = SB("BA", (128, NB * T))
    BB = SB("BB", (128, NB * T))
    WT = [SB(f"WT{i}", (128, NB * 128)) for i in range(2)]
    W8 = [SB(f"W8_{i}", (128, T)) for i in range(8)]
    Mt = SB("Mt", (128, T))
    Rt = SB("Rt", (128, T))
    XP = SB("XP", (128, T))
    E96 = SB("E96", (96, 2048))
    XPBC = SB("XPBC", (128, T))
    A_all = SB("A_all", (128, 256))
    BDT = SB("BDT", (128, 16))
    DV = SB("DV", (128, 16))
    GAM = SB("GAM", (128, 16))
    I0 = SB("I0", (128, 16))
    L0H = SB("L0H", (128, 16))
    L1H = SB("L1H", (128, 256))
    IDENT = SB("IDENT", (128, 128))
    ONES1 = SB("ONES1", (1, 128))
    ONESC = SB("ONESC", (128, 1))
    QSC = SB("QSC", (128, 128))
    CEPS5 = SB("CEPS5", (128, 1))
    CEPS6 = SB("CEPS6", (128, 1))
    QRS = [SB(f"QRS{i}", (128, 1)) for i in range(3)]
    ACOL = SB("ACOL", (128, 1))
    BCOL = SB("BCOL", (128, 1))
    DCOL = SB("DCOL", (128, 1))
    GCOL = SB("GCOL", (128, 1))
    CA = [SB(f"CA{i}", (128, 8)) for i in range(5)]
    CAT16 = SB("CAT16", (128, 16))
    WI = W8[1][:].bitcast(I32)
    STRV = W8[2][0:1, 0:T]
    STRM = W8[3][0:1, 0:T]
    STRS = W8[4][0:1, 0:T]
    PS = nc.alloc_psum_tensor("PS", [128, 8 * 512], F32).ap()

    def psb(i, w=512):
        return PS[:, i * 512: i * 512 + w]

    ct = Ctx(nc)

    def tap(nm, src_ap):
        if nm in tap_d:
            ct.wait("gpsimd", ct.sv)
            ct.wait("gpsimd", ct.sa)
            ct.wait("gpsimd", ct.sp)
            ct.wait("gpsimd", ct.gq)
            ct.wait("gpsimd", ct.sq)
            ct.dma("gpsimd", tap_d[nm][...], src_ap)

    def BAb(i):
        return BA[:, i * T:(i + 1) * T]

    def BBb(i):
        return BB[:, i * T:(i + 1) * T]

    def wtile(slot, kb, w=128):
        return WT[slot][:, kb * 128: kb * 128 + w]

    ct.dma("sync", IDENT[:], ident_i[:, :])
    ct.dma("sync", E96[:], e96_i[:, :])
    ct.dma("sync", A_all[:], alog_i[:, :])
    ct.dma("sync", BDT[:], bdt_i[:, :])
    ct.dma("sync", DV[:], dvec_i[:, :])
    ct.dma("sync", GAM[:], gam_i[:, :])
    ct.dma("sync", I0[:], i0_i[:, :])
    ct.wait("vector", ct.sq)
    ct.op("vector", lambda e: e.memset(ONES1[:], 1.0))
    ct.op("vector", lambda e: e.memset(CEPS5[:], 1e-5))
    ct.op("vector", lambda e: e.memset(CEPS6[:], 1e-6))
    ct.op("vector", lambda e: e.memset(ONESC[:], 1.0))
    ct.op("vector", lambda e: e.memset(L1H[:], 0.0))
    ct.op("vector", lambda e: e.tensor_copy(L0H[:], I0[:]))
    ct.wait("scalar", ct.sq)
    ct.op("scalar", lambda e: e.activation(W8[0][:, 0:256], A_all[:], AF.Exp))
    ct.wait("vector", ct.sa)
    ct.op("vector", lambda e: e.tensor_scalar_mul(A_all[:], W8[0][:, 0:256], -1.0))

    def mm_group(items, ps_ap):
        n = len(items)
        for j, (a, b) in enumerate(items):
            st, sp_ = (j == 0), (j == n - 1)
            if sp_:
                ct.op("tensor", lambda e, a=a, b=b, st=st: nc.tensor.matmul(
                    ps_ap, a, b, start=st, stop=True))
            else:
                ct.op_("tensor", lambda e, a=a, b=b, st=st: nc.tensor.matmul(
                    ps_ap, a, b, start=st, stop=False))

    def bcast_to(strip_ap_fn, dst_tile):
        for h in range(TH):
            ct.wait("tensor", ct.sa)
            ct.wait("tensor", ct.sv)
            ct.op("tensor", lambda e, s=strip_ap_fn(h): nc.tensor.matmul(
                psb(4), ONES1[0:1, :], s, start=True, stop=True))
            ct.wait("scalar", ct.sp)
            ct.op("scalar", lambda e, h=h, d=dst_tile: e.copy(d[:, h * 512:(h + 1) * 512], psb(4)))

    def ln_stats(src_fn, rms_only=False, eps=1e-5):
        for h in range(TH):
            for i in range(NB):
                src = src_fn(i, h)
                ct.wait("scalar", ct.sq)
                ct.wait("scalar", ct.sv)
                ct.wait("scalar", ct.sp)
                ct.op("scalar", lambda e, s=src: e.activation(W8[6][:, 0:512], s, AF.Square))
                if not rms_only:
                    ct.wait("tensor", ct.sq)
                    ct.wait("tensor", ct.sv)
                    ct.op("tensor", lambda e, s=src, st=(i == 0), lst=(i == NB - 1): nc.tensor.matmul(
                        PS[0:1, 6 * 512:7 * 512], ONESC[:, 0:1], s, start=st, stop=lst))
                ct.wait("tensor", ct.sa)
                ct.op("tensor", lambda e, st=(i == 0), lst=(i == NB - 1): nc.tensor.matmul(
                    PS[0:1, 7 * 512:8 * 512], ONESC[:, 0:1], W8[6][:, 0:512], start=st, stop=lst))
            sl = slice(h * 512, (h + 1) * 512)
            ct.wait("vector", ct.sp)
            if rms_only:
                ct.op("vector", lambda e, sl=sl: e.tensor_scalar_mul(
                    STRV[:, sl], PS[0:1, 7 * 512:8 * 512], 1.0 / H))
            else:
                ct.op("vector", lambda e, sl=sl: e.tensor_scalar_mul(
                    STRM[:, sl], PS[0:1, 6 * 512:7 * 512], 1.0 / H))
                ct.op("vector", lambda e, sl=sl: e.tensor_scalar_mul(
                    STRV[:, sl], PS[0:1, 7 * 512:8 * 512], 1.0 / H))
                ct.op("vector", lambda e, sl=sl: e.tensor_mul(
                    STRS[:, sl], STRM[:, sl], STRM[:, sl]))
                ct.op("vector", lambda e, sl=sl: e.tensor_sub(
                    STRV[:, sl], STRV[:, sl], STRS[:, sl]))
        ct.wait("scalar", ct.sv)
        ceps = CEPS5 if eps == 1e-5 else CEPS6
        ct.op("scalar", lambda e, ceps=ceps: e.activation(
            STRV[:, :], STRV[:, :], AF.Sqrt, bias=ceps[0:1, 0:1]))
        ct.wait("tensor", ct.sv)
        bcast_to(lambda h: STRV[:, h * 512:(h + 1) * 512], Rt)
        ct.wait("vector", ct.sa)
        ct.op("vector", lambda e: e.reciprocal(Rt[:], Rt[:]))
        if not rms_only:
            bcast_to(lambda h: STRM[:, h * 512:(h + 1) * 512], Mt)

    def ld_xblk(c, i, dst):
        ct.dma("sync", dst, x_in[i * 128:(i + 1) * 128, c * T:(c + 1) * T])

    QW, QB, QD = BA[:, 0:2048], BA[:, 2048:4096], BA[:, 4096:6144]
    QR, QDQ = BA[:, 6144:8192], BA[:, 8192:10240]
    WIA = W8[0][:].bitcast(I32)
    WIB = W8[1][:].bitcast(I32)
    g_amax, g_base, g_ib = BB[:, 0:32], BB[:, 32:64], BB[:, 64:96]
    g_best, g_bsc = BB[:, 96:128], BB[:, 128:160]
    g_err, g_cand, g_mask = BB[:, 160:192], BB[:, 192:224], BB[:, 224:256]

    def q_round(src, dst):
        ct.s_op("vector", lambda e: e.tensor_copy(WIA, src[:, 0:1024]))
        ct.s_op("vector", lambda e: e.tensor_copy(dst[:, 0:1024], WIA))
        ct.s_op("vector", lambda e: e.tensor_copy(WIB, src[:, 1024:2048]))
        ct.s_op("vector", lambda e: e.tensor_copy(dst[:, 1024:2048], WIB))

    ct.ctx_barrier()

    def turbo_quant(w_ap, nrows, ncols, dst_T):
        ntile = nrows // 128
        nun = ncols // 2048
        for ti in range(ntile):
            for un in range(nun):
                ct.s_dma("sync", QW, w_ap[ti * 128:(ti + 1) * 128, un * 2048:(un + 1) * 2048])
                vw = QW.rearrange("p (g k) -> p g k", k=64)
                ct.s_op("vector", lambda e, vw=vw: e.tensor_reduce(
                    g_amax.unsqueeze(2), vw, AXX, ALU.max, apply_absolute_value=True))
                ct.s_op("vector", lambda e: e.tensor_scalar_max(g_base, g_amax, 1e-8))
                ct.s_op("vector", lambda e: e.tensor_scalar_mul(g_base, g_base, 1.0 / QMAX))
                ct.s_op("vector", lambda e: e.reciprocal(g_ib, g_base))
                ct.s_op("vector", lambda e, vw=vw: e.tensor_tensor(
                    QB.rearrange("p (g k) -> p g k", k=64), vw,
                    g_ib.unsqueeze(2).broadcast_to((128, 32, 64)), ALU.mult))
                ct.s_op("vector", lambda e: e.memset(g_best, 3.4e38))
                ct.s_op("vector", lambda e: e.memset(g_bsc, 1.0))
                for s in SHRINKS:
                    c = float(np.float32(1.0) / np.float32(s))
                    s2 = float(np.float32(s) * np.float32(s))
                    ct.s_op("vector", lambda e, c=c: e.tensor_scalar_mul(QD, QB, c))
                    q_round(QD, QR)
                    ct.s_op("vector", lambda e: e.tensor_scalar(
                        QR, QR, QMAX, -QMAX, ALU.min, ALU.max))
                    ct.s_op("vector", lambda e: e.tensor_sub(QD, QR, QD))
                    ct.s_op("vector", lambda e: e.tensor_mul(QR, QD, QD))
                    ct.s_op("vector", lambda e: e.tensor_reduce(
                        g_err.unsqueeze(2),
                        QR.rearrange("p (g k) -> p g k", k=64), AXX, ALU.add))
                    ct.s_op("vector", lambda e, s2=s2: e.tensor_scalar_mul(g_err, g_err, s2))
                    ct.s_op("vector", lambda e, s=float(s): e.tensor_scalar_mul(g_cand, g_base, s))
                    ct.s_op("vector", lambda e: e.tensor_tensor(g_mask, g_err, g_best, ALU.is_lt))
                    ct.s_op("vector", lambda e: e.tensor_tensor(g_best, g_err, g_best, ALU.min))
                    ct.s_op("vector", lambda e: e.tensor_sub(g_cand, g_cand, g_bsc))
                    ct.s_op("vector", lambda e: e.tensor_mul(g_cand, g_cand, g_mask))
                    ct.s_op("vector", lambda e: e.tensor_add(g_bsc, g_bsc, g_cand))
                ct.s_op("vector", lambda e: e.reciprocal(g_ib, g_bsc))
                ct.s_op("vector", lambda e, vw=vw: e.tensor_tensor(
                    QD.rearrange("p (g k) -> p g k", k=64), vw,
                    g_ib.unsqueeze(2).broadcast_to((128, 32, 64)), ALU.mult))
                q_round(QD, QR)
                ct.s_op("vector", lambda e: e.tensor_scalar(
                    QR, QR, QMAX, -QMAX, ALU.min, ALU.max))
                ct.s_op("vector", lambda e: e.tensor_tensor(
                    QDQ.rearrange("p (g k) -> p g k", k=64),
                    QR.rearrange("p (g k) -> p g k", k=64),
                    g_bsc.unsqueeze(2).broadcast_to((128, 32, 64)), ALU.mult))
                for ib in range(16):
                    ct.s_op("tensor", lambda e, ib=ib: nc.tensor.transpose(
                        psb(5, 128), QDQ[:, ib * 128:(ib + 1) * 128], IDENT[:]))
                    ct.s_op("scalar", lambda e: e.copy(XPBC[:, 0:128], psb(5, 128)))
                    ct.s_dma("gpsimd",
                             dst_T[un * 2048 + 128 * ib: un * 2048 + 128 * (ib + 1),
                                   ti * 128:(ti + 1) * 128],
                             XPBC[:, 0:128])

    turbo_quant(wpj, H, FC, s_dpj)
    turbo_quant(wfc, FC, H, s_dfc)

    t_cur, t_acc = BB[:, 256:257], BB[:, 257:258]
    t_mean, t_ws, t_iws = BB[:, 258:259], BB[:, 259:260], BB[:, 260:261]
    for ti in range(NB):
        ct.s_dma("sync", QW, w1T[ti * 128:(ti + 1) * 128, :])
        ct.s_op("vector", lambda e: e.tensor_reduce(
            t_cur.unsqueeze(1), QW.unsqueeze(1), AXX, ALU.add, apply_absolute_value=True))
        if ti == 0:
            ct.s_op("vector", lambda e: e.tensor_copy(t_acc, t_cur))
        else:
            ct.s_op("vector", lambda e: e.tensor_add(t_acc, t_acc, t_cur))
    ct.s_op("tensor", lambda e: nc.tensor.matmul(
        PS[0:1, 5 * 512:5 * 512 + 1], t_acc, ONESC[:, 0:1], start=True, stop=True))
    ct.s_op("scalar", lambda e: e.copy(W8[7][0:1, 0:1], PS[0:1, 5 * 512:5 * 512 + 1]))
    ct.s_op("tensor", lambda e: nc.tensor.matmul(
        psb(5, 1), ONES1[0:1, :], W8[7][0:1, 0:1], start=True, stop=True))
    ct.s_op("vector", lambda e: e.tensor_copy(t_mean, psb(5, 1)))
    ct.s_op("vector", lambda e: e.tensor_scalar(
        t_mean, t_mean, 1.0 / (H * H), 1e-5, ALU.mult, ALU.add))
    ct.s_op("vector", lambda e: e.reciprocal(t_ws, t_mean))
    ct.s_op("vector", lambda e: e.reciprocal(t_iws, t_ws))
    for ti in range(NB):
        ct.s_dma("sync", QW, w1T[ti * 128:(ti + 1) * 128, :])
        ct.s_op("vector", lambda e: e.tensor_tensor(
            QD, QW, t_ws.broadcast_to((128, 2048)), ALU.mult))
        q_round(QD, QR)
        ct.s_op("vector", lambda e: e.tensor_scalar(
            QR, QR, 1.0, -1.0, ALU.min, ALU.max))
        ct.s_op("vector", lambda e: e.tensor_tensor(
            QDQ, QR, t_iws.broadcast_to((128, 2048)), ALU.mult))
        ct.s_dma("gpsimd", s_dw1[ti * 128:(ti + 1) * 128, :], QDQ)
    ct.s_join()
    tap("dpj", s_dpj[0:128, :])
    tap("dfc", s_dfc[0:128, :])
    tap("dw1", s_dw1[:, :])

    for c in range(NCH):
        cs = c * T

        def ln1_src(i, h):
            dst = W8[i % 2]
            ct.wait("sync", ct.sv, ct.sv.n)
            ct.wait("sync", ct.sa, ct.sa.n)
            ct.wait("sync", ct.sp, ct.sp.n)
            ct.dma("sync", dst[:, 0:512],
                   x_in[i * 128:(i + 1) * 128, cs + h * 512: cs + (h + 1) * 512])
            return dst[:, 0:512]
        ln_stats(ln1_src, rms_only=False, eps=1e-5)
        for i in range(NB):
            dst = W8[2 + i % 2]
            ct.wait("sync", ct.sv, ct.sv.n)
            ld_xblk(c, i, dst[:])
            ct.wait("vector", ct.sq)
            ct.wait("vector", ct.sa)
            ct.op("vector", lambda e, d=dst, i=i: e.tensor_sub(BAb(i), d[:], Mt[:]))
            ct.op("vector", lambda e, i=i: e.tensor_mul(BAb(i), BAb(i), Rt[:]))

        if c == 0:
            tap("mt", Mt[:, :])
            tap("rt", Rt[:, :])
            tap("xhat", BA[:, :])
        for rb in range(NB):
            slot = rb % 2
            for pi, wmat in enumerate((wf, wi, wv, wq)):
                ct.wait("sync", ct.sp, ct.sp.n)
                ct.dma("sync", WT[slot][:].rearrange("p (kb r) -> p kb r", kb=NB),
                       wslice_in(wmat, rb * 128, 128))
                ct.wait("tensor", ct.sq)
                ct.wait("tensor", ct.sa, ct.sa.n)
                if rb == 0 and pi == 0:
                    ct.wait("tensor", ct.sv, ct.sv.n)
                for h in range(TH):
                    items = [(wtile(slot, kb), BAb(kb)[:, h * 512:(h + 1) * 512]) for kb in range(NB)]
                    mm_group(items, psb(pi * 2 + h))
            ct.wait("scalar", ct.sp)
            ct.wait("scalar", ct.sv, ct.sv.n)
            for h in range(TH):
                sl = slice(h * 512, (h + 1) * 512)
                ct.op("scalar", lambda e, sl=sl, h=h: e.activation(W8[2][:, sl], psb(0 + h), AF.Tanh))
                ct.op("scalar", lambda e, sl=sl, h=h: e.activation(W8[3][:, sl], psb(2 + h), AF.Sigmoid))
                ct.op("scalar", lambda e, sl=sl, h=h: e.activation(W8[4][:, sl], psb(4 + h), AF.Silu))
                ct.op("scalar", lambda e, sl=sl, h=h: e.copy(W8[5][:, sl], psb(6 + h)))
            ct.wait("vector", ct.sa)
            ct.op("vector", lambda e: e.tensor_mul(W8[3][:], W8[3][:], W8[4][:]))
            ct.op("vector", lambda e, rb=rb: e.tensor_tensor_scan(
                W8[6][:], W8[2][:], W8[3][:], L0H[:, rb:rb + 1], ALU.mult, ALU.add))
            ct.wait("vector", ct.sv)
            ct.op("vector", lambda e, rb=rb: e.tensor_copy(L0H[:, rb:rb + 1], W8[6][:, T - 1:T]))
            ct.op("vector", lambda e: e.tensor_mul(W8[6][:], W8[6][:], W8[5][:]))
            ct.wait("scalar", ct.sv)
            ct.wait("scalar", ct.gq)
            ct.op("scalar", lambda e: e.activation(W8[7][:], W8[6][:], AF.Silu))
            ct.wait("gpsimd", ct.sa)
            ct.dma("gpsimd", s_rd[rb, :, :], W8[7][:])

        if c == 0:
            tap("rd", s_rd[:, :, :])
        for l in range(NB):
            slot = l % 2
            for pi, wmat in enumerate((wup, wgate)):
                ct.wait("sync", ct.sp, ct.sp.n)
                ct.dma("sync", WT[slot][:].rearrange("p (kb r) -> p kb r", kb=NB),
                       wslice_in(wmat, l * 128, 128))
                ct.wait("tensor", ct.sq)
                ct.wait("tensor", ct.sv, ct.sv.n)
                ct.wait("tensor", ct.sa, ct.sa.n)
                for h in range(TH):
                    items = [(wtile(slot, kb), BAb(kb)[:, h * 512:(h + 1) * 512]) for kb in range(NB)]
                    mm_group(items, psb(pi * 2 + h))
            ct.wait("scalar", ct.sp)
            for h in range(TH):
                sl = slice(h * 512, (h + 1) * 512)
                ct.op("scalar", lambda e, sl=sl, h=h: e.activation(W8[2][:, sl], psb(2 + h), AF.Silu))
            ct.wait("vector", ct.sa)
            for h in range(TH):
                sl = slice(h * 512, (h + 1) * 512)
                ct.op("vector", lambda e, l=l, sl=sl, h=h: e.tensor_tensor(
                    BBb(l)[:, sl], psb(0 + h), W8[2][:, sl], ALU.mult))

        ct.wait("sync", ct.gq)
        for kg in range(4):
            for j in range(4):
                ct.wait("sync", ct.sp, ct.sp.n)
                ct.wait("sync", ct.sa, ct.sa.n)
                ct.wait("sync", ct.sv, ct.sv.n)
                ct.dma("sync", W8[j][:], s_rd[kg * 4 + j, :, :])
            for ob in range(NB):
                ct.wait("sync", ct.sp, ct.sp.n)
                ct.dma("sync", WT[ob % 2][:, 0:512].rearrange("p (kb r) -> p kb r", kb=4),
                       wro[kg * 512:(kg + 1) * 512, ob * 128:(ob + 1) * 128]
                       .rearrange("(kb p) r -> p kb r", p=128))
                ct.wait("tensor", ct.sq)
                ct.wait("tensor", ct.sv, ct.sv.n)
                for h in range(TH):
                    items = [(WT[ob % 2][:, j * 128:(j + 1) * 128],
                              W8[j][:, h * 512:(h + 1) * 512]) for j in range(4)]
                    mm_group(items, psb(4 + h))
                ct.wait("vector", ct.sp)
                for h in range(TH):
                    sl = slice(h * 512, (h + 1) * 512)
                    if kg == 0:
                        ct.op("vector", lambda e, ob=ob, sl=sl, h=h: e.tensor_copy(
                            BAb(ob)[:, sl], psb(4 + h)))
                    else:
                        ct.op("vector", lambda e, ob=ob, sl=sl, h=h: e.tensor_add(
                            BAb(ob)[:, sl], BAb(ob)[:, sl], psb(4 + h)))

        for ob in range(NB):
            ct.wait("sync", ct.sp, ct.sp.n)
            ct.dma("sync", WT[ob % 2][:].rearrange("p (kb r) -> p kb r", kb=NB),
                   wslice_in(wdn, ob * 128, 128))
            ct.wait("tensor", ct.sq)
            ct.wait("tensor", ct.sv, ct.sv.n)
            for h in range(TH):
                items = [(wtile(ob % 2, kb), BBb(kb)[:, h * 512:(h + 1) * 512]) for kb in range(NB)]
                mm_group(items, psb(4 + h))
            ct.wait("vector", ct.sp)
            for h in range(TH):
                sl = slice(h * 512, (h + 1) * 512)
                ct.op("vector", lambda e, ob=ob, sl=sl, h=h: e.tensor_add(
                    BAb(ob)[:, sl], BAb(ob)[:, sl], psb(4 + h)))

        for i in range(NB):
            dst = W8[i % 2]
            ct.wait("sync", ct.sv, ct.sv.n)
            ld_xblk(c, i, dst[:])
            ct.wait("vector", ct.sq)
            ct.op("vector", lambda e, d=dst, i=i: e.tensor_add(BAb(i), BAb(i), d[:]))
            ct.wait("gpsimd", ct.sv)
            ct.dma("gpsimd", s_x1[i * 128:(i + 1) * 128, :], BAb(i))

        if c == 0:
            tap("x1", s_x1[:, :])
        ln_stats(lambda i, h: BAb(i)[:, h * 512:(h + 1) * 512], rms_only=False, eps=1e-5)
        ct.wait("vector", ct.gq)
        for i in range(NB):
            ct.wait("vector", ct.sa)
            ct.op("vector", lambda e, i=i: e.tensor_sub(BAb(i), BAb(i), Mt[:]))
            ct.op("vector", lambda e, i=i: e.tensor_mul(BAb(i), BAb(i), Rt[:]))

        if c == 0:
            tap("xhat2", BA[:, :])
        for og in range(NOG):
            slot = og % 2
            if og == 0:
                ct.wait("sync", ct.gq)
            ct.wait("sync", ct.sp, ct.sp.n)
            ct.dma("sync", WT[slot][:].rearrange("p (kb r) -> p kb r", kb=NB),
                   s_dfc[:, og * 128:(og + 1) * 128].rearrange("(kb p) r -> p kb r", p=128))
            ct.wait("tensor", ct.sq)
            ct.wait("tensor", ct.sa, ct.sa.n)
            if og == 0:
                ct.wait("tensor", ct.sv, ct.sv.n)
            for h in range(TH):
                items = [(wtile(slot, kb), BAb(kb)[:, h * 512:(h + 1) * 512]) for kb in range(NB)]
                mm_group(items, psb(0 + h))
            ct.wait("scalar", ct.sp)
            ct.wait("scalar", ct.gq)
            for h in range(TH):
                sl = slice(h * 512, (h + 1) * 512)
                ct.op("scalar", lambda e, sl=sl, h=h: e.activation(W8[4][:, sl], psb(0 + h), AF.Relu))
            ct.op("scalar", lambda e: e.activation(W8[6][:], W8[4][:], AF.Square))
            ct.wait("gpsimd", ct.sa)
            ct.dma("gpsimd", s_h2[og, :, :], W8[6][:])

        ct.wait("sync", ct.gq)
        for kg in range(4):
            for j in range(NB):
                ct.wait("sync", ct.sp, ct.sp.n)
                ct.wait("sync", ct.sv, ct.sv.n)
                ct.dma("sync", BBb(j), s_h2[kg * NB + j, :, :])
            for ob in range(NB):
                ct.wait("sync", ct.sp, ct.sp.n)
                ct.dma("sync", WT[ob % 2][:].rearrange("p (kb r) -> p kb r", kb=NB),
                       s_dpj[kg * 2048:(kg + 1) * 2048, ob * 128:(ob + 1) * 128]
                       .rearrange("(kb p) r -> p kb r", p=128))
                ct.wait("tensor", ct.sq)
                ct.wait("tensor", ct.sv, ct.sv.n)
                for h in range(TH):
                    items = [(wtile(ob % 2, kb), BBb(kb)[:, h * 512:(h + 1) * 512]) for kb in range(NB)]
                    mm_group(items, psb(4 + h))
                ct.wait("vector", ct.sp)
                for h in range(TH):
                    sl = slice(h * 512, (h + 1) * 512)
                    if kg == 0:
                        ct.op("vector", lambda e, ob=ob, sl=sl, h=h: e.tensor_copy(
                            BAb(ob)[:, sl], psb(4 + h)))
                    else:
                        ct.op("vector", lambda e, ob=ob, sl=sl, h=h: e.tensor_add(
                            BAb(ob)[:, sl], BAb(ob)[:, sl], psb(4 + h)))
        ct.wait("sync", ct.gq)
        for i in range(NB):
            dst = W8[i % 2]
            ct.wait("sync", ct.sv, ct.sv.n)
            ct.dma("sync", dst[:], s_x1[i * 128:(i + 1) * 128, :])
            ct.wait("vector", ct.sq)
            ct.op("vector", lambda e, d=dst, i=i: e.tensor_add(BAb(i), BAb(i), d[:]))
            ct.wait("gpsimd", ct.sv)
            ct.dma("gpsimd", s_x2[i * 128:(i + 1) * 128, :], BAb(i))

        if c == 0:
            tap("h2a", s_h2[0, :, :])
            tap("x2", s_x2[:, :])
        ln_stats(lambda i, h: BAb(i)[:, h * 512:(h + 1) * 512], rms_only=False, eps=1e-5)
        ct.wait("vector", ct.gq)
        for i in range(NB):
            ct.wait("vector", ct.sa)
            ct.op("vector", lambda e, i=i: e.tensor_sub(BAb(i), BAb(i), Mt[:]))
            ct.op("vector", lambda e, i=i: e.tensor_mul(BAb(i), BAb(i), Rt[:]))

        ct.wait("sync", ct.sp, ct.sp.n)
        ct.dma("sync", WT[0][:, 0:NB * 96].rearrange("p (kb r) -> p kb r", kb=NB),
               wxT[:, :].rearrange("(kb p) r -> p kb r", p=128))
        ct.wait("tensor", ct.sq)
        ct.wait("tensor", ct.sa, ct.sa.n)
        ct.wait("tensor", ct.sv, ct.sv.n)
        for h in range(TH):
            items = [(WT[0][:, kb * 96:(kb + 1) * 96], BAb(kb)[:, h * 512:(h + 1) * 512])
                     for kb in range(NB)]
            mm_group(items, PS[0:96, h * 512:(h + 1) * 512])
        ct.wait("scalar", ct.sp)
        ct.op("scalar", lambda e: e.copy(XP[0:96, 0:512], PS[0:96, 0:512]))
        ct.op("scalar", lambda e: e.copy(XP[0:96, 512:1024], PS[0:96, 512:1024]))
        ct.wait("gpsimd", ct.sa)
        ct.dma("gpsimd", s_bc[:, :], XP[80:96, :])
        ct.wait("gpsimd", ct.gq)
        ct.dma("gpsimd", XPBC[64:80, :], s_bc[:, :])
        ct.wait("sync", ct.sa, ct.sa.n)
        ct.dma("sync", WT[1][0:64, 0:2048].rearrange("p (db r) -> p db r", db=NB),
               wdtT[:, :].rearrange("k (db r) -> k db r", r=128))
        ct.wait("tensor", ct.sq)
        for db in range(NB):
            ct.wait("vector", ct.sa, ct.sa.n)
            ct.op("vector", lambda e, db=db: e.tensor_copy(BCOL[:], BDT[:, db:db + 1]))
            ct.wait("tensor", ct.sa, ct.sa.n)
            for h in range(TH):
                ct.op("tensor", lambda e, db=db, h=h: nc.tensor.matmul(
                    psb(0 + h), WT[1][0:64, db * 128:(db + 1) * 128],
                    XP[0:64, h * 512:(h + 1) * 512], start=True, stop=True))
            ct.wait("scalar", ct.sp)
            ct.wait("scalar", ct.sv)
            for h in range(TH):
                sl = slice(h * 512, (h + 1) * 512)
                ct.op("scalar", lambda e, sl=sl, h=h: e.activation(
                    W8[0][:, sl], psb(0 + h), AF.Abs, bias=BCOL[:, 0:1]))
                ct.op("scalar", lambda e, sl=sl: e.activation(
                    W8[0][:, sl], W8[0][:, sl], AF.Exp, scale=-1.0))
                ct.op("scalar", lambda e, sl=sl: e.activation(
                    W8[0][:, sl], W8[0][:, sl], AF.Ln, bias=1.0))
                ct.op("scalar", lambda e, sl=sl, h=h: e.activation(
                    W8[1][:, sl], psb(0 + h), AF.Relu, bias=BCOL[:, 0:1]))
            ct.wait("vector", ct.sa)
            ct.op("vector", lambda e, db=db: e.tensor_add(BBb(db), W8[0][:], W8[1][:]))

        if c == 0:
            tap("xp", XP[:, :])
            tap("dt", BB[:, :])
        for db in range(NB):
            ct.wait("vector", ct.sg, ct.sg.n)
            ct.op("vector", lambda e, db=db: e.tensor_mul(W8[0][:], BBb(db), BAb(db)))
            for n in range(16):
                idx = db * 16 + n
                ct.wait("tensor", ct.sv, ct.sv.n)
                ct.wait("tensor", ct.sg, ct.sg.n)
                ct.wait("tensor", ct.gq)
                for h in range(TH):
                    ct.op("tensor", lambda e, n=n, h=h: nc.tensor.matmul(
                        psb(0 + h), E96[:, n * 128:(n + 1) * 128],
                        XP[0:96, h * 512:(h + 1) * 512], start=True, stop=True))
                    ct.op("tensor", lambda e, n=n, h=h: nc.tensor.matmul(
                        psb(2 + h), E96[:, n * 128:(n + 1) * 128],
                        XPBC[0:96, h * 512:(h + 1) * 512], start=True, stop=True))
                ct.op("vector", lambda e, idx=idx: e.tensor_copy(ACOL[:], A_all[:, idx:idx + 1]))
                ct.wait("scalar", ct.sv)
                ct.op("scalar", lambda e, db=db: e.activation(
                    W8[1][:], BBb(db), AF.Exp, scale=ACOL[:, 0:1]))
                ct.wait("vector", ct.sp)
                ct.wait("vector", ct.sa)
                for h in range(TH):
                    sl = slice(h * 512, (h + 1) * 512)
                    ct.op("vector", lambda e, sl=sl, h=h: e.tensor_tensor(
                        W8[2][:, sl], W8[0][:, sl], psb(0 + h), ALU.mult))
                ct.op("vector", lambda e, idx=idx: e.tensor_tensor_scan(
                    W8[3][:], W8[1][:], W8[2][:], L1H[:, idx:idx + 1], ALU.mult, ALU.add))
                ct.wait("vector", ct.sv)
                ct.op("vector", lambda e, idx=idx: e.tensor_copy(L1H[:, idx:idx + 1], W8[3][:, T - 1:T]))
                ct.wait("vector", ct.sg)
                for h in range(TH):
                    sl = slice(h * 512, (h + 1) * 512)
                    ct.op("vector", lambda e, sl=sl, h=h: e.tensor_tensor(
                        W8[4][:, sl], W8[3][:, sl], psb(2 + h), ALU.mult))
                ct.wait("gpsimd", ct.sv)
                if n == 0:
                    ct.op("gpsimd", lambda e: e.tensor_copy(W8[5][:], W8[4][:]))
                else:
                    ct.op("gpsimd", lambda e: e.tensor_add(W8[5][:], W8[5][:], W8[4][:]))
            ct.wait("vector", ct.sg)
            ct.op("vector", lambda e, db=db: e.tensor_copy(DCOL[:], DV[:, db:db + 1]))
            ct.wait("vector", ct.sv)
            ct.op("vector", lambda e, db=db: e.tensor_tensor(
                W8[6][:], BAb(db), DCOL[:, 0:1].broadcast_to((128, T)), ALU.mult))
            ct.op("vector", lambda e: e.tensor_add(W8[5][:], W8[5][:], W8[6][:]))
            if db == 0:
                ct.wait("sync", ct.gq)
            ct.wait("sync", ct.sv, ct.sv.n)
            ct.dma("sync", W8[7][:], s_x2[db * 128:(db + 1) * 128, :])
            ct.wait("vector", ct.sq)
            ct.op("vector", lambda e, db=db: e.tensor_add(BAb(db), W8[5][:], W8[7][:]))
            ct.wait("gpsimd", ct.sv)
            ct.dma("gpsimd", s_x3[db * 128:(db + 1) * 128, :], BAb(db))

        if c == 0:
            tap("x3", s_x3[:, :])
        ln_stats(lambda i, h: BAb(i)[:, h * 512:(h + 1) * 512], rms_only=False, eps=1e-5)
        ct.wait("vector", ct.gq)
        for i in range(NB):
            ct.wait("vector", ct.sa)
            ct.op("vector", lambda e, i=i: e.tensor_sub(BAb(i), BAb(i), Mt[:]))
            ct.op("vector", lambda e, i=i: e.tensor_mul(BAb(i), BAb(i), Rt[:]))
        ln_stats(lambda i, h: BAb(i)[:, h * 512:(h + 1) * 512], rms_only=True, eps=1e-6)
        for i in range(NB):
            ct.wait("vector", ct.sa)
            ct.op("vector", lambda e, i=i: e.tensor_copy(GCOL[:], GAM[:, i:i + 1]))
            ct.op("vector", lambda e, i=i: e.tensor_mul(W8[5][:], BAb(i), Rt[:]))
            ct.wait("vector", ct.sv)
            ct.op("vector", lambda e, i=i: e.tensor_tensor(
                BBb(i), W8[5][:], GCOL[:, 0:1].broadcast_to((128, T)), ALU.mult))
        ct.wait("scalar", ct.sv)
        ct.op("scalar", lambda e: e.activation(W8[0][:], BBb(0), AF.Abs))
        for i in range(1, NB):
            ct.op("scalar", lambda e, i=i: e.activation(W8[5][:], BBb(i), AF.Abs))
            ct.wait("vector", ct.sa)
            ct.op("vector", lambda e: e.tensor_max(W8[6][:], W8[0][:], W8[5][:]))
            ct.op("vector", lambda e: e.tensor_copy(W8[0][:], W8[6][:]))
            if i < NB - 1:
                ct.wait("scalar", ct.sv)
        for j in range(8):
            ct.wait("tensor", ct.sv)
            ct.op("tensor", lambda e, j=j: nc.tensor.transpose(
                psb(5, 128), W8[0][:, j * 128:(j + 1) * 128], IDENT[:]))
            ct.wait("vector", ct.sp)
            ct.op("vector", lambda e: e.tensor_reduce(
                QRS[0][:].unsqueeze(1), psb(5, 128).unsqueeze(1), AXX, ALU.max))
            ct.wait("vector", ct.sv)
            ct.op("vector", lambda e, j=j: e.tensor_copy(CA[0][:, j:j + 1], QRS[0][:]))
        ct.wait("vector", ct.sv)
        ct.op("vector", lambda e: e.tensor_scalar(
            CA[1][:], CA[0][:], 1e-5, 0.0, ALU.add, ALU.add))
        ct.wait("vector", ct.sv)
        ct.op("vector", lambda e: e.reciprocal(CA[2][:], CA[1][:]))
        ct.wait("vector", ct.sv)
        ct.op("vector", lambda e: e.tensor_scalar_mul(CA[3][:], CA[2][:], 127.0))
        ct.op("vector", lambda e: e.tensor_scalar_mul(CA[4][:], CA[1][:], 1.0 / 127.0))
        ct.wait("vector", ct.sv)
        ct.op("vector", lambda e: e.tensor_copy(CAT16[:, 0:8], CA[3][:]))
        ct.op("vector", lambda e: e.tensor_copy(CAT16[:, 8:16], CA[4][:]))
        ct.wait("vector", ct.sv)
        ct.wait("tensor", ct.sv)
        ct.op("tensor", lambda e: nc.tensor.transpose(
            PS[0:16, 5 * 512:5 * 512 + 128], CAT16[:, :], IDENT[:]))
        ct.wait("vector", ct.sp)
        ct.op("vector", lambda e: e.tensor_copy(QSC[0:16, 0:128], PS[0:16, 5 * 512:5 * 512 + 128]))
        ct.wait("gpsimd", ct.sv)
        ct.dma("gpsimd", s_cols[:, :], QSC[0:16, 0:128])
        ct.wait("gpsimd", ct.gq)
        ct.dma("gpsimd", STRM[:, :], s_cols[0:8, :].rearrange("r c -> (r c)").unsqueeze(0))
        ct.dma("gpsimd", STRV[:, :], s_cols[8:16, :].rearrange("r c -> (r c)").unsqueeze(0))
        ct.wait("tensor", ct.sv)
        ct.wait("tensor", ct.gq)
        bcast_to(lambda h: STRM[:, h * 512:(h + 1) * 512], Mt)
        bcast_to(lambda h: STRV[:, h * 512:(h + 1) * 512], Rt)
        for i in range(NB):
            ct.wait("vector", ct.sa)
            ct.op("vector", lambda e, i=i: e.tensor_mul(W8[5][:], BBb(i), Mt[:]))
            ct.op("vector", lambda e: e.tensor_copy(WI, W8[5][:]))
            ct.op("vector", lambda e: e.tensor_copy(W8[6][:], WI))
            ct.op("vector", lambda e: e.tensor_scalar(
                W8[7][:], W8[6][:], 127.0, -128.0, ALU.min, ALU.max))
            ct.op("vector", lambda e, i=i: e.tensor_mul(BBb(i), W8[7][:], Rt[:]))

        for ob in range(NB):
            if ob == 0:
                ct.wait("sync", ct.gq)
            ct.wait("sync", ct.sp, ct.sp.n)
            ct.dma("sync", WT[ob % 2][:].rearrange("p (kb r) -> p kb r", kb=NB),
                   s_dw1[:, ob * 128:(ob + 1) * 128].rearrange("(kb p) r -> p kb r", p=128))
            ct.wait("tensor", ct.sq)
            ct.wait("tensor", ct.sv, ct.sv.n)
            for h in range(TH):
                items = [(wtile(ob % 2, kb), BBb(kb)[:, h * 512:(h + 1) * 512]) for kb in range(NB)]
                mm_group(items, psb(4 + h))
            ct.wait("sync", ct.sv, ct.sv.n)
            ct.dma("sync", W8[ob % 2][:], s_x3[ob * 128:(ob + 1) * 128, :])
            ct.wait("vector", ct.sq)
            ct.wait("vector", ct.sp)
            ct.wait("vector", ct.gq)
            for h in range(TH):
                sl = slice(h * 512, (h + 1) * 512)
                ct.op("vector", lambda e, ob=ob, sl=sl, h=h: e.tensor_add(
                    W8[2 + ob % 2][:, sl], W8[ob % 2][:, sl], psb(4 + h)))
            ct.wait("gpsimd", ct.sv)
            ct.dma("gpsimd", out_d[ob * 128:(ob + 1) * 128, cs:cs + T], W8[2 + ob % 2][:])

    ct.wait("gpsimd", ct.gq)
    ct.wait("gpsimd", ct.sq)

    with nc.Block() as block:
        @block.sync
        def _(eng):
            for fn in ct.plan["sync"]:
                fn(eng)

        @block.tensor
        def _(eng):
            for fn in ct.plan["tensor"]:
                fn(eng)

        @block.vector
        def _(eng):
            for fn in ct.plan["vector"]:
                fn(eng)

        @block.scalar
        def _(eng):
            for fn in ct.plan["scalar"]:
                fn(eng)

        @block.gpsimd
        def _(eng):
            for fn in ct.plan["gpsimd"]:
                fn(eng)

    return nc


def _make_e96():
    e = np.zeros((96, 2048), np.float32)
    for n in range(16):
        e[64 + n, n * 128:(n + 1) * 128] = 1.0
    return e


def make_in_maps(L0_attn_x, L0_attn_w_f, L0_attn_w_i, L0_attn_w_v, L0_attn_w_q, L0_attn_w_ro,
                 L0_attn_w_aux, L0_attn_w_up, L0_attn_w_gate, L0_attn_w_down, L0_attn_init_state,
                 L0_ffn_w_fc, L0_ffn_w_proj, L1_attn_W_x, L1_attn_W_dt, L1_attn_b_dt,
                 L1_attn_A_log, L1_attn_D, L1_ffn_gamma, L1_ffn_W):
    f32 = np.float32
    x = np.asarray(L0_attn_x, f32)

    def vec16(v):
        return np.ascontiguousarray(np.asarray(v, f32).reshape(16, 128).T)

    common = dict(
        wf=np.ascontiguousarray(np.asarray(L0_attn_w_f, f32)),
        wi=np.ascontiguousarray(np.asarray(L0_attn_w_i, f32)),
        wv=np.ascontiguousarray(np.asarray(L0_attn_w_v, f32)),
        wq=np.ascontiguousarray(np.asarray(L0_attn_w_q, f32)),
        wro=np.ascontiguousarray(np.asarray(L0_attn_w_ro, f32)),
        wup=np.ascontiguousarray(np.asarray(L0_attn_w_up, f32)),
        wgate=np.ascontiguousarray(np.asarray(L0_attn_w_gate, f32)),
        wdn=np.ascontiguousarray(np.asarray(L0_attn_w_down, f32)),
        wfc=np.ascontiguousarray(np.asarray(L0_ffn_w_fc, f32)),
        wpj=np.ascontiguousarray(np.asarray(L0_ffn_w_proj, f32)),
        wxT=np.ascontiguousarray(np.asarray(L1_attn_W_x, f32).T),
        wdtT=np.ascontiguousarray(np.asarray(L1_attn_W_dt, f32).T),
        bdt=vec16(L1_attn_b_dt),
        alog=np.ascontiguousarray(np.asarray(L1_attn_A_log, f32).reshape(16, 128, 16)
                                  .transpose(1, 0, 2).reshape(128, 256)),
        dvec=vec16(L1_attn_D),
        gam=vec16(L1_ffn_gamma),
        i0=vec16(L0_attn_init_state),
        w1T=np.ascontiguousarray(np.asarray(L1_ffn_W, f32).T),
        ident=np.eye(128, dtype=f32),
        e96=_make_e96(),
    )
    in_maps = []
    for core in range(8):
        b = core // 4
        m = dict(common)
        m["x"] = np.ascontiguousarray(x[b].T)
        in_maps.append(m)
    return in_maps


def kernel(**inputs):
    in_maps = make_in_maps(**inputs)
    nc = bass.Bass("TRN2", target_bir_lowering=False, debug=False)
    build(nc)
    res = run_bass_kernel_spmd(nc, in_maps, core_ids=list(range(8)))
    out0 = np.ascontiguousarray(res.results[0]["out"].T)
    out1 = np.ascontiguousarray(res.results[4]["out"].T)
    return np.stack([out0, out1]).astype(np.float32)


# version 07 (final)
# speedup vs baseline: 1.0806x; 1.0806x over previous
import numpy as np
import concourse.bass as bass
import concourse.mybir as mybir
from concourse.bass_utils import run_bass_kernel_spmd

F32 = mybir.dt.float32
I32 = mybir.dt.int32
AF = mybir.ActivationFunctionType
ALU = mybir.AluOpType
AXX = mybir.AxisListType.X

H = 2048
S = 4096
T = 1024
NCH = S // T
NB = H // 128
FC = 8192
NOG = FC // 128
SHRINKS = np.linspace(0.4, 1.0, 16)
QMAX = 15.0
TH = T // 512


class Sem:
    def __init__(self, nc, name):
        self.h = nc.alloc_semaphore(name=name)
        self.n = 0


class Ctx:
    ENGS = ("sync", "tensor", "vector", "scalar", "gpsimd")

    def __init__(self, nc):
        self.nc = nc
        self.plan = {e: [] for e in self.ENGS}
        self.sq = Sem(nc, "sq")
        self.gq = Sem(nc, "gq")
        self.sp = Sem(nc, "sp")
        self.sv = Sem(nc, "sv")
        self.sa = Sem(nc, "sa")
        self.sg = Sem(nc, "sg")
        self.tok = Sem(nc, "tok")

    def emit(self, eng, fn):
        self.plan[eng].append(fn)

    def wait(self, eng, sem, val=None):
        v = sem.n if val is None else val
        if v > 0:
            self.emit(eng, lambda e, s=sem.h, v=v: e.wait_ge(s, v))

    def op(self, eng, fn, sem=None):
        if sem is None:
            sem = {"tensor": self.sp, "vector": self.sv, "scalar": self.sa,
                   "gpsimd": self.sg}[eng]
        sem.n += 1
        self.emit(eng, lambda e, f=fn, s=sem.h: f(e).then_inc(s, 1))

    def op_(self, eng, fn):
        self.emit(eng, fn)

    def dma(self, eng, out, in_, sem=None):
        if sem is None:
            sem = self.sq if eng == "sync" else self.gq
        sem.n += 16
        self.emit(eng, lambda e, o=out, i=in_, s=sem.h: e.dma_start(out=o, in_=i).then_inc(s, 16))

    def s_op(self, eng, fn):
        w0 = self.tok.n
        self.tok.n += 1

        def run(e, f=fn, w0=w0, s=self.tok.h):
            if w0:
                e.wait_ge(s, w0)
            f(e).then_inc(s, 1)
        self.emit(eng, run)

    def s_dma(self, eng, out, in_):
        w0 = self.tok.n
        self.tok.n += 16

        def run(e, o=out, i=in_, w0=w0, s=self.tok.h):
            if w0:
                e.wait_ge(s, w0)
            e.dma_start(out=o, in_=i).then_inc(s, 16)
        self.emit(eng, run)

    def s_join(self):
        for eng in self.ENGS:
            self.emit(eng, lambda e, v=self.tok.n, s=self.tok.h: e.wait_ge(s, v))

    def ctx_barrier(self):
        for eng in self.ENGS:
            for sem in (self.sq, self.gq, self.sp, self.sv, self.sa, self.sg):
                self.wait(eng, sem)


def wslice_in(w, c0, ncol):
    return w[:, c0:c0 + ncol].rearrange("(kb p) r -> p kb r", p=128)


def build(nc, taps=None):
    def IN(name, shape):
        return nc.dram_tensor(name, list(shape), F32, kind="ExternalInput").ap()

    x_in = IN("x", (H, S))
    wf, wi, wv, wq = IN("wf", (H, H)), IN("wi", (H, H)), IN("wv", (H, H)), IN("wq", (H, H))
    wro, wup, wgate, wdn = IN("wro", (H, H)), IN("wup", (H, H)), IN("wgate", (H, H)), IN("wdn", (H, H))
    wfc, wpj = IN("wfc", (FC, H)), IN("wpj", (H, FC))
    wxT, wdtT = IN("wxT", (H, 96)), IN("wdtT", (64, H))
    bdt_i, alog_i = IN("bdt", (128, 16)), IN("alog", (128, 256))
    dvec_i, gam_i, i0_i = IN("dvec", (128, 16)), IN("gam", (128, 16)), IN("i0", (128, 16))
    w1T = IN("w1T", (H, H))
    ident_i = IN("ident", (128, 128))
    e96_i = IN("e96", (96, 2048))

    out_d = nc.dram_tensor("out", [H, S], F32, kind="ExternalOutput").ap()

    s_x1 = nc.dram_tensor("s_x1", [H, T], F32, kind="Internal").ap()
    s_x2 = nc.dram_tensor("s_x2", [H, T], F32, kind="Internal").ap()
    s_x3 = nc.dram_tensor("s_x3", [H, T], F32, kind="Internal").ap()
    s_rd = nc.dram_tensor("s_rd", [NB, 128, T], F32, kind="Internal").ap()
    s_h2 = nc.dram_tensor("s_h2", [NOG, 128, T], F32, kind="Internal").ap()
    s_dfc = nc.dram_tensor("s_dfc", [H, FC], F32, kind="Internal").ap()
    s_dpj = nc.dram_tensor("s_dpj", [FC, H], F32, kind="Internal").ap()
    s_dw1 = nc.dram_tensor("s_dw1", [H, H], F32, kind="Internal").ap()
    s_bc = nc.dram_tensor("s_bc", [16, T], F32, kind="Internal").ap()
    s_cols = nc.dram_tensor("s_cols", [16, 128], F32, kind="Internal").ap()

    tap_d = {}
    if taps:
        for nm, shape in taps:
            tap_d[nm] = nc.dram_tensor("tap_" + nm, list(shape), F32, kind="ExternalOutput").ap()

    def SB(name, shape, dtype=F32):
        return nc.alloc_sbuf_tensor(name, list(shape), dtype).ap()

    BA = SB("BA", (128, NB * T))
    BB = SB("BB", (128, NB * T))
    WT = [SB(f"WT{i}", (128, NB * 128)) for i in range(2)]
    W8 = [SB(f"W8_{i}", (128, T)) for i in range(8)]
    Mt = SB("Mt", (128, T))
    Rt = SB("Rt", (128, T))
    XP = SB("XP", (128, T))
    E96 = SB("E96", (96, 2048))
    XPBC = SB("XPBC", (128, T))
    A_all = SB("A_all", (128, 256))
    BDT = SB("BDT", (128, 16))
    DV = SB("DV", (128, 16))
    GAM = SB("GAM", (128, 16))
    I0 = SB("I0", (128, 16))
    L0H = SB("L0H", (128, 16))
    L1H = SB("L1H", (128, 256))
    IDENT = SB("IDENT", (128, 128))
    ONES1 = SB("ONES1", (1, 128))
    ONESC = SB("ONESC", (128, 1))
    QSC = SB("QSC", (128, 128))
    CEPS5 = SB("CEPS5", (128, 1))
    CEPS6 = SB("CEPS6", (128, 1))
    QRS = [SB(f"QRS{i}", (128, 1)) for i in range(3)]
    ACOL = SB("ACOL", (128, 1))
    BCOL = SB("BCOL", (128, 1))
    DCOL = SB("DCOL", (128, 1))
    GCOL = SB("GCOL", (128, 1))
    CA = [SB(f"CA{i}", (128, 8)) for i in range(5)]
    CAT16 = SB("CAT16", (128, 16))
    WI = W8[1][:].bitcast(I32)
    STRV = W8[2][0:1, 0:T]
    STRM = W8[3][0:1, 0:T]
    STRS = W8[4][0:1, 0:T]
    PS = nc.alloc_psum_tensor("PS", [128, 8 * 512], F32).ap()

    def psb(i, w=512):
        return PS[:, i * 512: i * 512 + w]

    ct = Ctx(nc)

    def tap(nm, src_ap):
        if nm in tap_d:
            ct.wait("gpsimd", ct.sv)
            ct.wait("gpsimd", ct.sa)
            ct.wait("gpsimd", ct.sp)
            ct.wait("gpsimd", ct.gq)
            ct.wait("gpsimd", ct.sq)
            ct.dma("gpsimd", tap_d[nm][...], src_ap)

    def BAb(i):
        return BA[:, i * T:(i + 1) * T]

    def BBb(i):
        return BB[:, i * T:(i + 1) * T]

    def wtile(slot, kb, w=128):
        return WT[slot][:, kb * 128: kb * 128 + w]

    ct.dma("sync", IDENT[:], ident_i[:, :])
    ct.dma("sync", E96[:], e96_i[:, :])
    ct.dma("sync", A_all[:], alog_i[:, :])
    ct.dma("sync", BDT[:], bdt_i[:, :])
    ct.dma("sync", DV[:], dvec_i[:, :])
    ct.dma("sync", GAM[:], gam_i[:, :])
    ct.dma("sync", I0[:], i0_i[:, :])
    ct.wait("vector", ct.sq)
    ct.op("vector", lambda e: e.memset(ONES1[:], 1.0))
    ct.op("vector", lambda e: e.memset(CEPS5[:], 1e-5))
    ct.op("vector", lambda e: e.memset(CEPS6[:], 1e-6))
    ct.op("vector", lambda e: e.memset(ONESC[:], 1.0))
    ct.op("vector", lambda e: e.memset(L1H[:], 0.0))
    ct.op("vector", lambda e: e.tensor_copy(L0H[:], I0[:]))
    ct.wait("scalar", ct.sq)
    ct.op("scalar", lambda e: e.activation(W8[0][:, 0:256], A_all[:], AF.Exp))
    ct.wait("vector", ct.sa)
    ct.op("vector", lambda e: e.tensor_scalar_mul(A_all[:], W8[0][:, 0:256], -1.0))

    def mm_group(items, ps_ap):
        n = len(items)
        for j, (a, b) in enumerate(items):
            st, sp_ = (j == 0), (j == n - 1)
            if sp_:
                ct.op("tensor", lambda e, a=a, b=b, st=st: nc.tensor.matmul(
                    ps_ap, a, b, start=st, stop=True))
            else:
                ct.op_("tensor", lambda e, a=a, b=b, st=st: nc.tensor.matmul(
                    ps_ap, a, b, start=st, stop=False))

    def bcast_to(strip_ap_fn, dst_tile):
        for h in range(TH):
            ct.wait("tensor", ct.sa)
            ct.wait("tensor", ct.sv)
            ct.op("tensor", lambda e, s=strip_ap_fn(h): nc.tensor.matmul(
                psb(4), ONES1[0:1, :], s, start=True, stop=True))
            ct.wait("scalar", ct.sp)
            ct.op("scalar", lambda e, h=h, d=dst_tile: e.copy(d[:, h * 512:(h + 1) * 512], psb(4)))

    def ln_stats(src_fn, rms_only=False, eps=1e-5):
        for h in range(TH):
            for i in range(NB):
                src = src_fn(i, h)
                ct.wait("scalar", ct.sq)
                ct.wait("scalar", ct.sv)
                ct.wait("scalar", ct.sp)
                ct.op("scalar", lambda e, s=src: e.activation(W8[6][:, 0:512], s, AF.Square))
                if not rms_only:
                    ct.wait("tensor", ct.sq)
                    ct.wait("tensor", ct.sv)
                    ct.op("tensor", lambda e, s=src, st=(i == 0), lst=(i == NB - 1): nc.tensor.matmul(
                        PS[0:1, 6 * 512:7 * 512], ONESC[:, 0:1], s, start=st, stop=lst))
                ct.wait("tensor", ct.sa)
                ct.op("tensor", lambda e, st=(i == 0), lst=(i == NB - 1): nc.tensor.matmul(
                    PS[0:1, 7 * 512:8 * 512], ONESC[:, 0:1], W8[6][:, 0:512], start=st, stop=lst))
            sl = slice(h * 512, (h + 1) * 512)
            ct.wait("vector", ct.sp)
            if rms_only:
                ct.op("vector", lambda e, sl=sl: e.tensor_scalar_mul(
                    STRV[:, sl], PS[0:1, 7 * 512:8 * 512], 1.0 / H))
            else:
                ct.op("vector", lambda e, sl=sl: e.tensor_scalar_mul(
                    STRM[:, sl], PS[0:1, 6 * 512:7 * 512], 1.0 / H))
                ct.op("vector", lambda e, sl=sl: e.tensor_scalar_mul(
                    STRV[:, sl], PS[0:1, 7 * 512:8 * 512], 1.0 / H))
                ct.op("vector", lambda e, sl=sl: e.tensor_mul(
                    STRS[:, sl], STRM[:, sl], STRM[:, sl]))
                ct.op("vector", lambda e, sl=sl: e.tensor_sub(
                    STRV[:, sl], STRV[:, sl], STRS[:, sl]))
        ct.wait("scalar", ct.sv)
        ceps = CEPS5 if eps == 1e-5 else CEPS6
        ct.op("scalar", lambda e, ceps=ceps: e.activation(
            STRV[:, :], STRV[:, :], AF.Sqrt, bias=ceps[0:1, 0:1]))
        ct.wait("tensor", ct.sv)
        bcast_to(lambda h: STRV[:, h * 512:(h + 1) * 512], Rt)
        ct.wait("vector", ct.sa)
        ct.op("vector", lambda e: e.reciprocal(Rt[:], Rt[:]))
        if not rms_only:
            bcast_to(lambda h: STRM[:, h * 512:(h + 1) * 512], Mt)

    def ld_xblk(c, i, dst):
        ct.dma("sync", dst, x_in[i * 128:(i + 1) * 128, c * T:(c + 1) * T])

    QW, QB, QD = BA[:, 0:2048], BA[:, 2048:4096], BA[:, 4096:6144]
    QR, QDQ = BA[:, 6144:8192], BA[:, 8192:10240]
    WIA = W8[0][:].bitcast(I32)
    WIB = W8[1][:].bitcast(I32)
    g_amax, g_ib = BB[:, 0:128], BB[:, 128:256]
    g_best, g_bsc = BB[:, 256:384], BB[:, 384:512]
    g_err, g_cand, g_mask = BB[:, 512:640], BB[:, 640:768], BB[:, 768:896]

    def q_round(src, dst):
        ct.op("vector", lambda e: e.tensor_copy(WIA, src[:, 0:1024]))
        ct.op("vector", lambda e: e.tensor_copy(dst[:, 0:1024], WIA))
        ct.op("vector", lambda e: e.tensor_copy(WIB, src[:, 1024:2048]))
        ct.op("vector", lambda e: e.tensor_copy(dst[:, 1024:2048], WIB))

    ct.ctx_barrier()

    def turbo_quant(w_ap, nrows, ncols, dst_T):
        ntile = nrows // 128
        nun = ncols // 2048
        for ti in range(ntile):
            for un in range(nun):
                ct.wait("sync", ct.sv, ct.sv.n)
                ct.dma("sync", QW, w_ap[ti * 128:(ti + 1) * 128, un * 2048:(un + 1) * 2048])
                vw = QW.rearrange("p (g k) -> p g k", k=64)
                ct.wait("vector", ct.sq)
                ct.wait("vector", ct.sp, ct.sp.n)
                ct.op("vector", lambda e, vw=vw: e.tensor_reduce(
                    g_amax[:, 0:32].unsqueeze(2), vw, AXX, ALU.max, apply_absolute_value=True))
                ct.wait("vector", ct.sv)
                ct.op("vector", lambda e: e.tensor_scalar(
                    g_amax, g_amax, 1e-8, 1.0 / QMAX, ALU.max, ALU.mult))
                ct.op("vector", lambda e: e.reciprocal(g_ib, g_amax))
                ct.wait("vector", ct.sv)
                ct.op("vector", lambda e, vw=vw: e.tensor_tensor(
                    QB.rearrange("p (g k) -> p g k", k=64), vw,
                    g_ib[:, 0:32].unsqueeze(2).broadcast_to((128, 32, 64)), ALU.mult))
                ct.op("vector", lambda e: e.memset(g_best, 3.4e38))
                ct.op("vector", lambda e: e.memset(g_bsc, 1.0))
                for s in SHRINKS:
                    c = float(np.float32(1.0) / np.float32(s))
                    s2 = float(np.float32(s) * np.float32(s))
                    ct.op("vector", lambda e, c=c: e.tensor_scalar_mul(QD, QB, c))
                    q_round(QD, QR)
                    ct.op("vector", lambda e: e.tensor_scalar(
                        QR, QR, QMAX, -QMAX, ALU.min, ALU.max))
                    ct.op("vector", lambda e: e.tensor_sub(QD, QR, QD))
                    ct.op("vector", lambda e: e.tensor_mul(QR, QD, QD))
                    ct.op("vector", lambda e: e.tensor_reduce(
                        g_err[:, 0:32].unsqueeze(2),
                        QR.rearrange("p (g k) -> p g k", k=64), AXX, ALU.add))
                    ct.wait("vector", ct.sv)
                    ct.op("vector", lambda e, s2=s2: e.tensor_scalar_mul(g_err, g_err, s2))
                    ct.op("vector", lambda e, s=float(s): e.tensor_scalar_mul(g_cand, g_amax, s))
                    ct.op("vector", lambda e: e.tensor_tensor(g_mask, g_err, g_best, ALU.is_lt))
                    ct.op("vector", lambda e: e.tensor_tensor(g_best, g_err, g_best, ALU.min))
                    ct.op("vector", lambda e: e.tensor_sub(g_cand, g_cand, g_bsc))
                    ct.op("vector", lambda e: e.tensor_mul(g_cand, g_cand, g_mask))
                    ct.op("vector", lambda e: e.tensor_add(g_bsc, g_bsc, g_cand))
                ct.op("vector", lambda e: e.reciprocal(g_ib, g_bsc))
                ct.wait("vector", ct.sv)
                ct.op("vector", lambda e, vw=vw: e.tensor_tensor(
                    QD.rearrange("p (g k) -> p g k", k=64), vw,
                    g_ib[:, 0:32].unsqueeze(2).broadcast_to((128, 32, 64)), ALU.mult))
                q_round(QD, QR)
                ct.op("vector", lambda e: e.tensor_scalar(
                    QR, QR, QMAX, -QMAX, ALU.min, ALU.max))
                ct.op("vector", lambda e: e.tensor_tensor(
                    QDQ.rearrange("p (g k) -> p g k", k=64),
                    QR.rearrange("p (g k) -> p g k", k=64),
                    g_bsc[:, 0:32].unsqueeze(2).broadcast_to((128, 32, 64)), ALU.mult))
                for ib in range(16):
                    ct.wait("tensor", ct.sv)
                    ct.wait("tensor", ct.sa, ct.sa.n)
                    ct.op("tensor", lambda e, ib=ib: nc.tensor.transpose(
                        psb(5, 128), QDQ[:, ib * 128:(ib + 1) * 128], IDENT[:]))
                    ct.wait("scalar", ct.sp)
                    ct.wait("scalar", ct.gq)
                    ct.op("scalar", lambda e: e.copy(XPBC[:, 0:128], psb(5, 128)))
                    ct.wait("gpsimd", ct.sa)
                    ct.dma("gpsimd",
                           dst_T[un * 2048 + 128 * ib: un * 2048 + 128 * (ib + 1),
                                 ti * 128:(ti + 1) * 128],
                           XPBC[:, 0:128])

    turbo_quant(wpj, H, FC, s_dpj)
    turbo_quant(wfc, FC, H, s_dfc)
    ct.ctx_barrier()

    t_cur, t_acc = BB[:, 256:257], BB[:, 257:258]
    t_mean, t_ws, t_iws = BB[:, 258:259], BB[:, 259:260], BB[:, 260:261]
    for ti in range(NB):
        ct.s_dma("sync", QW, w1T[ti * 128:(ti + 1) * 128, :])
        ct.s_op("vector", lambda e: e.tensor_reduce(
            t_cur.unsqueeze(1), QW.unsqueeze(1), AXX, ALU.add, apply_absolute_value=True))
        if ti == 0:
            ct.s_op("vector", lambda e: e.tensor_copy(t_acc, t_cur))
        else:
            ct.s_op("vector", lambda e: e.tensor_add(t_acc, t_acc, t_cur))
    ct.s_op("tensor", lambda e: nc.tensor.matmul(
        PS[0:1, 5 * 512:5 * 512 + 1], t_acc, ONESC[:, 0:1], start=True, stop=True))
    ct.s_op("scalar", lambda e: e.copy(W8[7][0:1, 0:1], PS[0:1, 5 * 512:5 * 512 + 1]))
    ct.s_op("tensor", lambda e: nc.tensor.matmul(
        psb(5, 1), ONES1[0:1, :], W8[7][0:1, 0:1], start=True, stop=True))
    ct.s_op("vector", lambda e: e.tensor_copy(t_mean, psb(5, 1)))
    ct.s_op("vector", lambda e: e.tensor_scalar(
        t_mean, t_mean, 1.0 / (H * H), 1e-5, ALU.mult, ALU.add))
    ct.s_op("vector", lambda e: e.reciprocal(t_ws, t_mean))
    ct.s_op("vector", lambda e: e.reciprocal(t_iws, t_ws))
    for ti in range(NB):
        ct.s_dma("sync", QW, w1T[ti * 128:(ti + 1) * 128, :])
        ct.s_op("vector", lambda e: e.tensor_tensor(
            QD, QW, t_ws.broadcast_to((128, 2048)), ALU.mult))
        q_round(QD, QR)
        ct.s_op("vector", lambda e: e.tensor_scalar(
            QR, QR, 1.0, -1.0, ALU.min, ALU.max))
        ct.s_op("vector", lambda e: e.tensor_tensor(
            QDQ, QR, t_iws.broadcast_to((128, 2048)), ALU.mult))
        ct.s_dma("gpsimd", s_dw1[ti * 128:(ti + 1) * 128, :], QDQ)
    ct.s_join()
    tap("dpj", s_dpj[0:128, :])
    tap("dfc", s_dfc[0:128, :])
    tap("dw1", s_dw1[:, :])

    for c in range(NCH):
        cs = c * T

        def ln1_src(i, h):
            dst = W8[i % 2]
            ct.wait("sync", ct.sv, ct.sv.n)
            ct.wait("sync", ct.sa, ct.sa.n)
            ct.wait("sync", ct.sp, ct.sp.n)
            ct.dma("sync", dst[:, 0:512],
                   x_in[i * 128:(i + 1) * 128, cs + h * 512: cs + (h + 1) * 512])
            return dst[:, 0:512]
        ln_stats(ln1_src, rms_only=False, eps=1e-5)
        for i in range(NB):
            dst = W8[2 + i % 2]
            ct.wait("sync", ct.sv, ct.sv.n)
            ld_xblk(c, i, dst[:])
            ct.wait("vector", ct.sq)
            ct.wait("vector", ct.sa)
            ct.op("vector", lambda e, d=dst, i=i: e.tensor_sub(BAb(i), d[:], Mt[:]))
            ct.op("vector", lambda e, i=i: e.tensor_mul(BAb(i), BAb(i), Rt[:]))

        if c == 0:
            tap("mt", Mt[:, :])
            tap("rt", Rt[:, :])
            tap("xhat", BA[:, :])
        for rb in range(NB):
            slot = rb % 2
            for pi, wmat in enumerate((wf, wi, wv, wq)):
                ct.wait("sync", ct.sp, ct.sp.n)
                ct.dma("sync", WT[slot][:].rearrange("p (kb r) -> p kb r", kb=NB),
                       wslice_in(wmat, rb * 128, 128))
                ct.wait("tensor", ct.sq)
                ct.wait("tensor", ct.sa, ct.sa.n)
                if rb == 0 and pi == 0:
                    ct.wait("tensor", ct.sv, ct.sv.n)
                for h in range(TH):
                    items = [(wtile(slot, kb), BAb(kb)[:, h * 512:(h + 1) * 512]) for kb in range(NB)]
                    mm_group(items, psb(pi * 2 + h))
            ct.wait("scalar", ct.sp)
            ct.wait("scalar", ct.sv, ct.sv.n)
            for h in range(TH):
                sl = slice(h * 512, (h + 1) * 512)
                ct.op("scalar", lambda e, sl=sl, h=h: e.activation(W8[2][:, sl], psb(0 + h), AF.Tanh))
                ct.op("scalar", lambda e, sl=sl, h=h: e.activation(W8[3][:, sl], psb(2 + h), AF.Sigmoid))
                ct.op("scalar", lambda e, sl=sl, h=h: e.activation(W8[4][:, sl], psb(4 + h), AF.Silu))
                ct.op("scalar", lambda e, sl=sl, h=h: e.copy(W8[5][:, sl], psb(6 + h)))
            ct.wait("vector", ct.sa)
            ct.op("vector", lambda e: e.tensor_mul(W8[3][:], W8[3][:], W8[4][:]))
            ct.op("vector", lambda e, rb=rb: e.tensor_tensor_scan(
                W8[6][:], W8[2][:], W8[3][:], L0H[:, rb:rb + 1], ALU.mult, ALU.add))
            ct.wait("vector", ct.sv)
            ct.op("vector", lambda e, rb=rb: e.tensor_copy(L0H[:, rb:rb + 1], W8[6][:, T - 1:T]))
            ct.op("vector", lambda e: e.tensor_mul(W8[6][:], W8[6][:], W8[5][:]))
            ct.wait("scalar", ct.sv)
            ct.wait("scalar", ct.gq)
            ct.op("scalar", lambda e: e.activation(W8[7][:], W8[6][:], AF.Silu))
            ct.wait("gpsimd", ct.sa)
            ct.dma("gpsimd", s_rd[rb, :, :], W8[7][:])

        if c == 0:
            tap("rd", s_rd[:, :, :])
        for l in range(NB):
            slot = l % 2
            for pi, wmat in enumerate((wup, wgate)):
                ct.wait("sync", ct.sp, ct.sp.n)
                ct.dma("sync", WT[slot][:].rearrange("p (kb r) -> p kb r", kb=NB),
                       wslice_in(wmat, l * 128, 128))
                ct.wait("tensor", ct.sq)
                ct.wait("tensor", ct.sv, ct.sv.n)
                ct.wait("tensor", ct.sa, ct.sa.n)
                for h in range(TH):
                    items = [(wtile(slot, kb), BAb(kb)[:, h * 512:(h + 1) * 512]) for kb in range(NB)]
                    mm_group(items, psb(pi * 2 + h))
            ct.wait("scalar", ct.sp)
            for h in range(TH):
                sl = slice(h * 512, (h + 1) * 512)
                ct.op("scalar", lambda e, sl=sl, h=h: e.activation(W8[2][:, sl], psb(2 + h), AF.Silu))
            ct.wait("vector", ct.sa)
            for h in range(TH):
                sl = slice(h * 512, (h + 1) * 512)
                ct.op("vector", lambda e, l=l, sl=sl, h=h: e.tensor_tensor(
                    BBb(l)[:, sl], psb(0 + h), W8[2][:, sl], ALU.mult))

        ct.wait("sync", ct.gq)
        for kg in range(4):
            for j in range(4):
                ct.wait("sync", ct.sp, ct.sp.n)
                ct.wait("sync", ct.sa, ct.sa.n)
                ct.wait("sync", ct.sv, ct.sv.n)
                ct.dma("sync", W8[j][:], s_rd[kg * 4 + j, :, :])
            for ob in range(NB):
                ct.wait("sync", ct.sp, ct.sp.n)
                ct.dma("sync", WT[ob % 2][:, 0:512].rearrange("p (kb r) -> p kb r", kb=4),
                       wro[kg * 512:(kg + 1) * 512, ob * 128:(ob + 1) * 128]
                       .rearrange("(kb p) r -> p kb r", p=128))
                ct.wait("tensor", ct.sq)
                ct.wait("tensor", ct.sv, ct.sv.n)
                for h in range(TH):
                    items = [(WT[ob % 2][:, j * 128:(j + 1) * 128],
                              W8[j][:, h * 512:(h + 1) * 512]) for j in range(4)]
                    mm_group(items, psb(4 + h))
                ct.wait("vector", ct.sp)
                for h in range(TH):
                    sl = slice(h * 512, (h + 1) * 512)
                    if kg == 0:
                        ct.op("vector", lambda e, ob=ob, sl=sl, h=h: e.tensor_copy(
                            BAb(ob)[:, sl], psb(4 + h)))
                    else:
                        ct.op("vector", lambda e, ob=ob, sl=sl, h=h: e.tensor_add(
                            BAb(ob)[:, sl], BAb(ob)[:, sl], psb(4 + h)))

        for ob in range(NB):
            ct.wait("sync", ct.sp, ct.sp.n)
            ct.dma("sync", WT[ob % 2][:].rearrange("p (kb r) -> p kb r", kb=NB),
                   wslice_in(wdn, ob * 128, 128))
            ct.wait("tensor", ct.sq)
            ct.wait("tensor", ct.sv, ct.sv.n)
            for h in range(TH):
                items = [(wtile(ob % 2, kb), BBb(kb)[:, h * 512:(h + 1) * 512]) for kb in range(NB)]
                mm_group(items, psb(4 + h))
            ct.wait("vector", ct.sp)
            for h in range(TH):
                sl = slice(h * 512, (h + 1) * 512)
                ct.op("vector", lambda e, ob=ob, sl=sl, h=h: e.tensor_add(
                    BAb(ob)[:, sl], BAb(ob)[:, sl], psb(4 + h)))

        for i in range(NB):
            dst = W8[i % 2]
            ct.wait("sync", ct.sv, ct.sv.n)
            ld_xblk(c, i, dst[:])
            ct.wait("vector", ct.sq)
            ct.op("vector", lambda e, d=dst, i=i: e.tensor_add(BAb(i), BAb(i), d[:]))
            ct.wait("gpsimd", ct.sv)
            ct.dma("gpsimd", s_x1[i * 128:(i + 1) * 128, :], BAb(i))

        if c == 0:
            tap("x1", s_x1[:, :])
        ln_stats(lambda i, h: BAb(i)[:, h * 512:(h + 1) * 512], rms_only=False, eps=1e-5)
        ct.wait("vector", ct.gq)
        for i in range(NB):
            ct.wait("vector", ct.sa)
            ct.op("vector", lambda e, i=i: e.tensor_sub(BAb(i), BAb(i), Mt[:]))
            ct.op("vector", lambda e, i=i: e.tensor_mul(BAb(i), BAb(i), Rt[:]))

        if c == 0:
            tap("xhat2", BA[:, :])
        for og in range(NOG):
            slot = og % 2
            if og == 0:
                ct.wait("sync", ct.gq)
            ct.wait("sync", ct.sp, ct.sp.n)
            ct.dma("sync", WT[slot][:].rearrange("p (kb r) -> p kb r", kb=NB),
                   s_dfc[:, og * 128:(og + 1) * 128].rearrange("(kb p) r -> p kb r", p=128))
            ct.wait("tensor", ct.sq)
            ct.wait("tensor", ct.sa, ct.sa.n)
            if og == 0:
                ct.wait("tensor", ct.sv, ct.sv.n)
            for h in range(TH):
                items = [(wtile(slot, kb), BAb(kb)[:, h * 512:(h + 1) * 512]) for kb in range(NB)]
                mm_group(items, psb(0 + h))
            ct.wait("scalar", ct.sp)
            ct.wait("scalar", ct.gq)
            for h in range(TH):
                sl = slice(h * 512, (h + 1) * 512)
                ct.op("scalar", lambda e, sl=sl, h=h: e.activation(W8[4][:, sl], psb(0 + h), AF.Relu))
            ct.op("scalar", lambda e: e.activation(W8[6][:], W8[4][:], AF.Square))
            ct.wait("gpsimd", ct.sa)
            ct.dma("gpsimd", s_h2[og, :, :], W8[6][:])

        ct.wait("sync", ct.gq)
        for kg in range(4):
            for j in range(NB):
                ct.wait("sync", ct.sp, ct.sp.n)
                ct.wait("sync", ct.sv, ct.sv.n)
                ct.dma("sync", BBb(j), s_h2[kg * NB + j, :, :])
            for ob in range(NB):
                ct.wait("sync", ct.sp, ct.sp.n)
                ct.dma("sync", WT[ob % 2][:].rearrange("p (kb r) -> p kb r", kb=NB),
                       s_dpj[kg * 2048:(kg + 1) * 2048, ob * 128:(ob + 1) * 128]
                       .rearrange("(kb p) r -> p kb r", p=128))
                ct.wait("tensor", ct.sq)
                ct.wait("tensor", ct.sv, ct.sv.n)
                for h in range(TH):
                    items = [(wtile(ob % 2, kb), BBb(kb)[:, h * 512:(h + 1) * 512]) for kb in range(NB)]
                    mm_group(items, psb(4 + h))
                ct.wait("vector", ct.sp)
                for h in range(TH):
                    sl = slice(h * 512, (h + 1) * 512)
                    if kg == 0:
                        ct.op("vector", lambda e, ob=ob, sl=sl, h=h: e.tensor_copy(
                            BAb(ob)[:, sl], psb(4 + h)))
                    else:
                        ct.op("vector", lambda e, ob=ob, sl=sl, h=h: e.tensor_add(
                            BAb(ob)[:, sl], BAb(ob)[:, sl], psb(4 + h)))
        ct.wait("sync", ct.gq)
        for i in range(NB):
            dst = W8[i % 2]
            ct.wait("sync", ct.sv, ct.sv.n)
            ct.dma("sync", dst[:], s_x1[i * 128:(i + 1) * 128, :])
            ct.wait("vector", ct.sq)
            ct.op("vector", lambda e, d=dst, i=i: e.tensor_add(BAb(i), BAb(i), d[:]))
            ct.wait("gpsimd", ct.sv)
            ct.dma("gpsimd", s_x2[i * 128:(i + 1) * 128, :], BAb(i))

        if c == 0:
            tap("h2a", s_h2[0, :, :])
            tap("x2", s_x2[:, :])
        ln_stats(lambda i, h: BAb(i)[:, h * 512:(h + 1) * 512], rms_only=False, eps=1e-5)
        ct.wait("vector", ct.gq)
        for i in range(NB):
            ct.wait("vector", ct.sa)
            ct.op("vector", lambda e, i=i: e.tensor_sub(BAb(i), BAb(i), Mt[:]))
            ct.op("vector", lambda e, i=i: e.tensor_mul(BAb(i), BAb(i), Rt[:]))

        ct.wait("sync", ct.sp, ct.sp.n)
        ct.dma("sync", WT[0][:, 0:NB * 96].rearrange("p (kb r) -> p kb r", kb=NB),
               wxT[:, :].rearrange("(kb p) r -> p kb r", p=128))
        ct.wait("tensor", ct.sq)
        ct.wait("tensor", ct.sa, ct.sa.n)
        ct.wait("tensor", ct.sv, ct.sv.n)
        for h in range(TH):
            items = [(WT[0][:, kb * 96:(kb + 1) * 96], BAb(kb)[:, h * 512:(h + 1) * 512])
                     for kb in range(NB)]
            mm_group(items, PS[0:96, h * 512:(h + 1) * 512])
        ct.wait("scalar", ct.sp)
        ct.op("scalar", lambda e: e.copy(XP[0:96, 0:512], PS[0:96, 0:512]))
        ct.op("scalar", lambda e: e.copy(XP[0:96, 512:1024], PS[0:96, 512:1024]))
        ct.wait("gpsimd", ct.sa)
        ct.dma("gpsimd", s_bc[:, :], XP[80:96, :])
        ct.wait("gpsimd", ct.gq)
        ct.dma("gpsimd", XPBC[64:80, :], s_bc[:, :])
        ct.wait("sync", ct.sa, ct.sa.n)
        ct.dma("sync", WT[1][0:64, 0:2048].rearrange("p (db r) -> p db r", db=NB),
               wdtT[:, :].rearrange("k (db r) -> k db r", r=128))
        ct.wait("tensor", ct.sq)
        for db in range(NB):
            ct.wait("vector", ct.sa, ct.sa.n)
            ct.op("vector", lambda e, db=db: e.tensor_copy(BCOL[:], BDT[:, db:db + 1]))
            ct.wait("tensor", ct.sa, ct.sa.n)
            for h in range(TH):
                ct.op("tensor", lambda e, db=db, h=h: nc.tensor.matmul(
                    psb(0 + h), WT[1][0:64, db * 128:(db + 1) * 128],
                    XP[0:64, h * 512:(h + 1) * 512], start=True, stop=True))
            ct.wait("scalar", ct.sp)
            ct.wait("scalar", ct.sv)
            for h in range(TH):
                sl = slice(h * 512, (h + 1) * 512)
                ct.op("scalar", lambda e, sl=sl, h=h: e.activation(
                    W8[0][:, sl], psb(0 + h), AF.Abs, bias=BCOL[:, 0:1]))
                ct.op("scalar", lambda e, sl=sl: e.activation(
                    W8[0][:, sl], W8[0][:, sl], AF.Exp, scale=-1.0))
                ct.op("scalar", lambda e, sl=sl: e.activation(
                    W8[0][:, sl], W8[0][:, sl], AF.Ln, bias=1.0))
                ct.op("scalar", lambda e, sl=sl, h=h: e.activation(
                    W8[1][:, sl], psb(0 + h), AF.Relu, bias=BCOL[:, 0:1]))
            ct.wait("vector", ct.sa)
            ct.op("vector", lambda e, db=db: e.tensor_add(BBb(db), W8[0][:], W8[1][:]))

        if c == 0:
            tap("xp", XP[:, :])
            tap("dt", BB[:, :])
        for db in range(NB):
            ct.wait("vector", ct.sg, ct.sg.n)
            ct.op("vector", lambda e, db=db: e.tensor_mul(W8[0][:], BBb(db), BAb(db)))
            for n in range(16):
                idx = db * 16 + n
                ct.wait("tensor", ct.sv, ct.sv.n)
                ct.wait("tensor", ct.sg, ct.sg.n)
                ct.wait("tensor", ct.gq)
                for h in range(TH):
                    ct.op("tensor", lambda e, n=n, h=h: nc.tensor.matmul(
                        psb(0 + h), E96[:, n * 128:(n + 1) * 128],
                        XP[0:96, h * 512:(h + 1) * 512], start=True, stop=True))
                    ct.op("tensor", lambda e, n=n, h=h: nc.tensor.matmul(
                        psb(2 + h), E96[:, n * 128:(n + 1) * 128],
                        XPBC[0:96, h * 512:(h + 1) * 512], start=True, stop=True))
                ct.op("vector", lambda e, idx=idx: e.tensor_copy(ACOL[:], A_all[:, idx:idx + 1]))
                ct.wait("scalar", ct.sv)
                ct.op("scalar", lambda e, db=db: e.activation(
                    W8[1][:], BBb(db), AF.Exp, scale=ACOL[:, 0:1]))
                ct.wait("vector", ct.sp)
                ct.wait("vector", ct.sa)
                for h in range(TH):
                    sl = slice(h * 512, (h + 1) * 512)
                    ct.op("vector", lambda e, sl=sl, h=h: e.tensor_tensor(
                        W8[2][:, sl], W8[0][:, sl], psb(0 + h), ALU.mult))
                ct.op("vector", lambda e, idx=idx: e.tensor_tensor_scan(
                    W8[3][:], W8[1][:], W8[2][:], L1H[:, idx:idx + 1], ALU.mult, ALU.add))
                ct.wait("vector", ct.sv)
                ct.op("vector", lambda e, idx=idx: e.tensor_copy(L1H[:, idx:idx + 1], W8[3][:, T - 1:T]))
                ct.wait("vector", ct.sg)
                for h in range(TH):
                    sl = slice(h * 512, (h + 1) * 512)
                    ct.op("vector", lambda e, sl=sl, h=h: e.tensor_tensor(
                        W8[4][:, sl], W8[3][:, sl], psb(2 + h), ALU.mult))
                ct.wait("gpsimd", ct.sv)
                if n == 0:
                    ct.op("gpsimd", lambda e: e.tensor_copy(W8[5][:], W8[4][:]))
                else:
                    ct.op("gpsimd", lambda e: e.tensor_add(W8[5][:], W8[5][:], W8[4][:]))
            ct.wait("vector", ct.sg)
            ct.op("vector", lambda e, db=db: e.tensor_copy(DCOL[:], DV[:, db:db + 1]))
            ct.wait("vector", ct.sv)
            ct.op("vector", lambda e, db=db: e.tensor_tensor(
                W8[6][:], BAb(db), DCOL[:, 0:1].broadcast_to((128, T)), ALU.mult))
            ct.op("vector", lambda e: e.tensor_add(W8[5][:], W8[5][:], W8[6][:]))
            if db == 0:
                ct.wait("sync", ct.gq)
            ct.wait("sync", ct.sv, ct.sv.n)
            ct.dma("sync", W8[7][:], s_x2[db * 128:(db + 1) * 128, :])
            ct.wait("vector", ct.sq)
            ct.op("vector", lambda e, db=db: e.tensor_add(BAb(db), W8[5][:], W8[7][:]))
            ct.wait("gpsimd", ct.sv)
            ct.dma("gpsimd", s_x3[db * 128:(db + 1) * 128, :], BAb(db))

        if c == 0:
            tap("x3", s_x3[:, :])
        ln_stats(lambda i, h: BAb(i)[:, h * 512:(h + 1) * 512], rms_only=False, eps=1e-5)
        ct.wait("vector", ct.gq)
        for i in range(NB):
            ct.wait("vector", ct.sa)
            ct.op("vector", lambda e, i=i: e.tensor_sub(BAb(i), BAb(i), Mt[:]))
            ct.op("vector", lambda e, i=i: e.tensor_mul(BAb(i), BAb(i), Rt[:]))
        ln_stats(lambda i, h: BAb(i)[:, h * 512:(h + 1) * 512], rms_only=True, eps=1e-6)
        for i in range(NB):
            ct.wait("vector", ct.sa)
            ct.op("vector", lambda e, i=i: e.tensor_copy(GCOL[:], GAM[:, i:i + 1]))
            ct.op("vector", lambda e, i=i: e.tensor_mul(W8[5][:], BAb(i), Rt[:]))
            ct.wait("vector", ct.sv)
            ct.op("vector", lambda e, i=i: e.tensor_tensor(
                BBb(i), W8[5][:], GCOL[:, 0:1].broadcast_to((128, T)), ALU.mult))
        ct.wait("scalar", ct.sv)
        ct.op("scalar", lambda e: e.activation(W8[0][:], BBb(0), AF.Abs))
        for i in range(1, NB):
            ct.op("scalar", lambda e, i=i: e.activation(W8[5][:], BBb(i), AF.Abs))
            ct.wait("vector", ct.sa)
            ct.op("vector", lambda e: e.tensor_max(W8[6][:], W8[0][:], W8[5][:]))
            ct.op("vector", lambda e: e.tensor_copy(W8[0][:], W8[6][:]))
            if i < NB - 1:
                ct.wait("scalar", ct.sv)
        for j in range(8):
            ct.wait("tensor", ct.sv)
            ct.op("tensor", lambda e, j=j: nc.tensor.transpose(
                psb(5, 128), W8[0][:, j * 128:(j + 1) * 128], IDENT[:]))
            ct.wait("vector", ct.sp)
            ct.op("vector", lambda e: e.tensor_reduce(
                QRS[0][:].unsqueeze(1), psb(5, 128).unsqueeze(1), AXX, ALU.max))
            ct.wait("vector", ct.sv)
            ct.op("vector", lambda e, j=j: e.tensor_copy(CA[0][:, j:j + 1], QRS[0][:]))
        ct.wait("vector", ct.sv)
        ct.op("vector", lambda e: e.tensor_scalar(
            CA[1][:], CA[0][:], 1e-5, 0.0, ALU.add, ALU.add))
        ct.wait("vector", ct.sv)
        ct.op("vector", lambda e: e.reciprocal(CA[2][:], CA[1][:]))
        ct.wait("vector", ct.sv)
        ct.op("vector", lambda e: e.tensor_scalar_mul(CA[3][:], CA[2][:], 127.0))
        ct.op("vector", lambda e: e.tensor_scalar_mul(CA[4][:], CA[1][:], 1.0 / 127.0))
        ct.wait("vector", ct.sv)
        ct.op("vector", lambda e: e.tensor_copy(CAT16[:, 0:8], CA[3][:]))
        ct.op("vector", lambda e: e.tensor_copy(CAT16[:, 8:16], CA[4][:]))
        ct.wait("vector", ct.sv)
        ct.wait("tensor", ct.sv)
        ct.op("tensor", lambda e: nc.tensor.transpose(
            PS[0:16, 5 * 512:5 * 512 + 128], CAT16[:, :], IDENT[:]))
        ct.wait("vector", ct.sp)
        ct.op("vector", lambda e: e.tensor_copy(QSC[0:16, 0:128], PS[0:16, 5 * 512:5 * 512 + 128]))
        ct.wait("gpsimd", ct.sv)
        ct.dma("gpsimd", s_cols[:, :], QSC[0:16, 0:128])
        ct.wait("gpsimd", ct.gq)
        ct.dma("gpsimd", STRM[:, :], s_cols[0:8, :].rearrange("r c -> (r c)").unsqueeze(0))
        ct.dma("gpsimd", STRV[:, :], s_cols[8:16, :].rearrange("r c -> (r c)").unsqueeze(0))
        ct.wait("tensor", ct.sv)
        ct.wait("tensor", ct.gq)
        bcast_to(lambda h: STRM[:, h * 512:(h + 1) * 512], Mt)
        bcast_to(lambda h: STRV[:, h * 512:(h + 1) * 512], Rt)
        for i in range(NB):
            ct.wait("vector", ct.sa)
            ct.op("vector", lambda e, i=i: e.tensor_mul(W8[5][:], BBb(i), Mt[:]))
            ct.op("vector", lambda e: e.tensor_copy(WI, W8[5][:]))
            ct.op("vector", lambda e: e.tensor_copy(W8[6][:], WI))
            ct.op("vector", lambda e: e.tensor_scalar(
                W8[7][:], W8[6][:], 127.0, -128.0, ALU.min, ALU.max))
            ct.op("vector", lambda e, i=i: e.tensor_mul(BBb(i), W8[7][:], Rt[:]))

        for ob in range(NB):
            if ob == 0:
                ct.wait("sync", ct.gq)
            ct.wait("sync", ct.sp, ct.sp.n)
            ct.dma("sync", WT[ob % 2][:].rearrange("p (kb r) -> p kb r", kb=NB),
                   s_dw1[:, ob * 128:(ob + 1) * 128].rearrange("(kb p) r -> p kb r", p=128))
            ct.wait("tensor", ct.sq)
            ct.wait("tensor", ct.sv, ct.sv.n)
            for h in range(TH):
                items = [(wtile(ob % 2, kb), BBb(kb)[:, h * 512:(h + 1) * 512]) for kb in range(NB)]
                mm_group(items, psb(4 + h))
            ct.wait("sync", ct.sv, ct.sv.n)
            ct.dma("sync", W8[ob % 2][:], s_x3[ob * 128:(ob + 1) * 128, :])
            ct.wait("vector", ct.sq)
            ct.wait("vector", ct.sp)
            ct.wait("vector", ct.gq)
            for h in range(TH):
                sl = slice(h * 512, (h + 1) * 512)
                ct.op("vector", lambda e, ob=ob, sl=sl, h=h: e.tensor_add(
                    W8[2 + ob % 2][:, sl], W8[ob % 2][:, sl], psb(4 + h)))
            ct.wait("gpsimd", ct.sv)
            ct.dma("gpsimd", out_d[ob * 128:(ob + 1) * 128, cs:cs + T], W8[2 + ob % 2][:])

    ct.wait("gpsimd", ct.gq)
    ct.wait("gpsimd", ct.sq)

    with nc.Block() as block:
        @block.sync
        def _(eng):
            for fn in ct.plan["sync"]:
                fn(eng)

        @block.tensor
        def _(eng):
            for fn in ct.plan["tensor"]:
                fn(eng)

        @block.vector
        def _(eng):
            for fn in ct.plan["vector"]:
                fn(eng)

        @block.scalar
        def _(eng):
            for fn in ct.plan["scalar"]:
                fn(eng)

        @block.gpsimd
        def _(eng):
            for fn in ct.plan["gpsimd"]:
                fn(eng)

    return nc


def _make_e96():
    e = np.zeros((96, 2048), np.float32)
    for n in range(16):
        e[64 + n, n * 128:(n + 1) * 128] = 1.0
    return e


def make_in_maps(L0_attn_x, L0_attn_w_f, L0_attn_w_i, L0_attn_w_v, L0_attn_w_q, L0_attn_w_ro,
                 L0_attn_w_aux, L0_attn_w_up, L0_attn_w_gate, L0_attn_w_down, L0_attn_init_state,
                 L0_ffn_w_fc, L0_ffn_w_proj, L1_attn_W_x, L1_attn_W_dt, L1_attn_b_dt,
                 L1_attn_A_log, L1_attn_D, L1_ffn_gamma, L1_ffn_W):
    f32 = np.float32
    x = np.asarray(L0_attn_x, f32)

    def vec16(v):
        return np.ascontiguousarray(np.asarray(v, f32).reshape(16, 128).T)

    common = dict(
        wf=np.ascontiguousarray(np.asarray(L0_attn_w_f, f32)),
        wi=np.ascontiguousarray(np.asarray(L0_attn_w_i, f32)),
        wv=np.ascontiguousarray(np.asarray(L0_attn_w_v, f32)),
        wq=np.ascontiguousarray(np.asarray(L0_attn_w_q, f32)),
        wro=np.ascontiguousarray(np.asarray(L0_attn_w_ro, f32)),
        wup=np.ascontiguousarray(np.asarray(L0_attn_w_up, f32)),
        wgate=np.ascontiguousarray(np.asarray(L0_attn_w_gate, f32)),
        wdn=np.ascontiguousarray(np.asarray(L0_attn_w_down, f32)),
        wfc=np.ascontiguousarray(np.asarray(L0_ffn_w_fc, f32)),
        wpj=np.ascontiguousarray(np.asarray(L0_ffn_w_proj, f32)),
        wxT=np.ascontiguousarray(np.asarray(L1_attn_W_x, f32).T),
        wdtT=np.ascontiguousarray(np.asarray(L1_attn_W_dt, f32).T),
        bdt=vec16(L1_attn_b_dt),
        alog=np.ascontiguousarray(np.asarray(L1_attn_A_log, f32).reshape(16, 128, 16)
                                  .transpose(1, 0, 2).reshape(128, 256)),
        dvec=vec16(L1_attn_D),
        gam=vec16(L1_ffn_gamma),
        i0=vec16(L0_attn_init_state),
        w1T=np.ascontiguousarray(np.asarray(L1_ffn_W, f32).T),
        ident=np.eye(128, dtype=f32),
        e96=_make_e96(),
    )
    in_maps = []
    for core in range(8):
        b = core // 4
        m = dict(common)
        m["x"] = np.ascontiguousarray(x[b].T)
        in_maps.append(m)
    return in_maps


def kernel(**inputs):
    in_maps = make_in_maps(**inputs)
    nc = bass.Bass("TRN2", target_bir_lowering=False, debug=False)
    build(nc)
    res = run_bass_kernel_spmd(nc, in_maps, core_ids=list(range(8)))
    out0 = np.ascontiguousarray(res.results[0]["out"].T)
    out1 = np.ascontiguousarray(res.results[4]["out"].T)
    return np.stack([out0, out1]).astype(np.float32)
